# version 04 (final)
# baseline (speedup 1.0000x reference)
import numpy as np
import concourse.bass as bass
import concourse.mybir as mybir
from concourse.bass_utils import run_bass_kernel_spmd
from contextlib import ExitStack

F32 = mybir.dt.float32
BF16 = mybir.dt.bfloat16
I32 = mybir.dt.int32
AF = mybir.ActivationFunctionType
ALU = mybir.AluOpType
AX = mybir.AxisListType

T = 2048
D = 2048
H = 8192
CH = 256
NCHUNK = T // CH
NSS = NCHUNK + 2
KD = D // 128
MH = H // 128
TWO_PI_HI = float(np.float32(6.2831855))
TWO_PI_LO = float(np.float64(2 * np.pi) - np.float64(np.float32(6.2831855)))
INV_2PI = float(np.float32(1.0 / (2 * np.pi)))


class PB:

    SEMKEYS = ("t", "v", "s", "g", "q", "sd", "gd")
    OWNER = {"t": "t", "v": "v", "s": "s", "g": "g", "q": "q", "sd": "s", "gd": "g"}

    def __init__(self):
        self.items = {e: [] for e in "tvsgq"}
        self.cnt = {k: 0 for k in self.SEMKEYS}
        self.lastw = {}

    def need(self, stream, tok):
        if tok is None:
            return
        key, val = tok
        if key in "tvsg" and len(key) == 1 and self.OWNER[key] == stream:
            return
        if self.lastw.get((stream, key), 0) >= val:
            return
        self.lastw[(stream, key)] = val
        self.items[stream].append(("w", key, val))

    def run(self, stream, fn):
        self.items[stream].append(("c", fn))

    def mile(self, stream, fn=None):
        if stream == "t":
            self.items[stream].append(("ci", fn, "t"))
            self.cnt["t"] += 1
            return ("t", self.cnt["t"])
        if fn is not None:
            self.items[stream].append(("c", fn))
        self.items[stream].append(("ci", lambda e: e.drain(), stream))
        self.cnt[stream] += 1
        return (stream, self.cnt[stream])

    def dma(self, stream, fn):
        key = {"q": "q", "s": "sd", "g": "gd"}[stream]
        self.items[stream].append(("di", fn, key))
        self.cnt[key] += 16
        return (key, self.cnt[key])

    def finalize(self):
        runend = {}
        for stream, key in (("q", "q"), ("s", "sd"), ("g", "gd")):
            cur = 0
            run = []
            for it in self.items[stream]:
                if it[0] == "w":
                    for v in run:
                        runend[(key, v)] = cur
                    run = []
                elif it[0] == "di" and it[2] == key:
                    cur += 16
                    run.append(cur)
            for v in run:
                runend[(key, v)] = cur
        for stream in self.items:
            fixed = []
            for it in self.items[stream]:
                if it[0] == "w" and (it[1], it[2]) in runend:
                    fixed.append(("w", it[1], runend[(it[1], it[2])]))
                else:
                    fixed.append(it)
            self.items[stream] = fixed


def build_program(dbg=False):
    nc = bass.Bass("TRN2", target_bir_lowering=False,
                   detect_race_conditions=False)

    x_fm = nc.dram_tensor("x_fm", [D, T], F32, kind="ExternalInput").ap()
    w_in = nc.dram_tensor("w_in", [D, D], F32, kind="ExternalInput").ap()
    w_out = nc.dram_tensor("w_out", [D, D], F32, kind="ExternalInput").ap()
    w_g = nc.dram_tensor("w_g", [H, D], F32, kind="ExternalInput").ap()
    w_u = nc.dram_tensor("w_u", [H, D], F32, kind="ExternalInput").ap()
    w_d = nc.dram_tensor("w_d", [D, H], F32, kind="ExternalInput").ap()
    wgx_in = nc.dram_tensor("wgx", [128, 16, 128], F32, kind="ExternalInput").ap()
    wga_in = nc.dram_tensor("wga", [128, 16, 128], F32, kind="ExternalInput").ap()
    bgx_in = nc.dram_tensor("bgx", [128, 8], F32, kind="ExternalInput").ap()
    bga_in = nc.dram_tensor("bga", [128, 8], F32, kind="ExternalInput").ap()
    bin_in = nc.dram_tensor("bin", [128, 16], F32, kind="ExternalInput").ap()
    bout_in = nc.dram_tensor("bout", [128, 16], F32, kind="ExternalInput").ap()
    arel_in = nc.dram_tensor("arel", [128, 8], F32, kind="ExternalInput").ap()
    aimg_in = nc.dram_tensor("aimg", [128, 8], F32, kind="ExternalInput").ap()
    y_fm = nc.dram_tensor("y_fm", [D, T], F32, kind="ExternalOutput").ap()
    dbg_t = {}
    if dbg:
        for nm, shp, dt_ in (("dbg_xq", [128, KD, CH], BF16),
                             ("dbg_xh", [128, KD, CH], BF16),
                             ("dbg_gx", [128, 8, CH], BF16),
                             ("dbg_ga", [128, 8, CH], BF16),
                             ("dbg_hm", [128, KD, CH], BF16),
                             ("dbg_r1", [128, KD, CH], F32),
                             ("dbg_r2", [128, KD, CH], F32),
                             ("dbg_l2b", [128, KD, CH], BF16),
                             ("dbg_p", [MH, 128, CH], BF16),
                             ("dbg_pr", [128, 4, CH], BF16),
                             ("dbg_wd", [128, 2, 8, 128], BF16),
                             ("dbg_r2c6", [128, KD, CH], F32),
                             ("dbg_xqc6", [128, KD, CH], BF16),
                             ("dbg_pr8", [128, 4, CH], BF16),
                             ("dbg_wg8", [128, KD, 128], BF16),
                             ("dbg_p8", [4, 128, CH], BF16),
                             ("dbg_pw8", [128, CH], BF16),
                             ("dbg_gb8", [128, CH], F32),
                             ("dbg_r1a", [NCHUNK, 128, KD, CH], F32),
                             ("dbg_r2a", [NCHUNK, 128, KD, CH], F32),
                             ("dbg_hma", [NCHUNK, 128, KD, CH], BF16),
                             ("dbg_p9", [2, MH, 128, CH], BF16),
                             ("dbg_xqc7", [128, KD, CH], BF16)):
            dbg_t[nm] = nc.dram_tensor(nm, shp, dt_, kind="ExternalOutput").ap()

    wgT_s = nc.dram_tensor("wgT_s", [MH, 128, KD * 128], BF16).ap()
    wuT_s = nc.dram_tensor("wuT_s", [MH, 128, KD * 128], BF16).ap()
    wdT_s = nc.dram_tensor("wdT_s", [2, MH, 128, 8 * 128], BF16).ap()
    winB_s = nc.dram_tensor("winB_s", [KD, 128, KD * 128], BF16).ap()
    woutB_s = nc.dram_tensor("woutB_s", [KD, 128, KD * 128], BF16).ap()
    p_scr = nc.dram_tensor("p_scr", [2, MH, 128, 512], BF16).ap()

    es = ExitStack()

    def sb(name, shape, dtype):
        return es.enter_context(nc.sbuf_tensor(name, shape, dtype))

    def psum(name, shape, dtype=F32):
        return es.enter_context(nc.psum_tensor(name, shape, dtype))

    slot = sb("slot", [128, 3, KD, CH], F32)
    xq = sb("xq", [128, 2, KD, 512], BF16)
    wgT = sb("wgT", [128, 2, KD, 128], BF16)
    wuT = sb("wuT", [128, 2, KD, 128], BF16)
    wdT = sb("wdT", [128, 2, 8, 128], BF16)
    wio = sb("wio", [128, 2, KD, 128], BF16)
    pw = sb("pw", [128, 4, 512], BF16)
    pr = sb("pr", [128, 4, 512], BF16)
    gbuf = sb("gbuf", [128, 2, 512], F32)
    wdprep = gbuf[:].bitcast(BF16).rearrange("p a (k t) -> p (a k) t", t=128)
    arena = sb("arena", [128, 12288], BF16)
    xh = arena[:, 0:4096].rearrange("p (k t) -> p k t", t=CH)
    l2b = arena[:, 4096:8192].rearrange("p (k t) -> p k t", t=CH)
    gx_t = arena[:, 8192:10240].rearrange("p (k t) -> p k t", t=CH)
    ga_t = arena[:, 10240:12288].rearrange("p (k t) -> p k t", t=CH)
    aren_f = arena[:].bitcast(F32)
    prep_f = [aren_f[:, 0:2048], aren_f[:, 2048:4096]]
    prep_b = [arena[:, 8192:10240], arena[:, 10240:12288]]
    hm = sb("hm", [128, KD, CH], BF16)
    st_ = [sb(f"sct{i}", [128, 8, 128], F32) for i in range(7)]
    sti = st_[3][:].bitcast(I32)
    zeros_f = sb("zeros_f", [128, 128], F32)
    thC = sb("thC", [128, 8], F32)
    grC = sb("grC", [128, 8], F32)
    giC = sb("giC", [128, 8], F32)
    Mx = sb("Mx", [128, CH], F32)
    Mn = sb("Mn", [128, CH], F32)
    m_b = sb("m_b", [128, CH], F32)
    v_b = sb("v_b", [128, CH], F32)
    R_b = sb("R_b", [128, CH], F32)
    A_b = sb("A_b", [128, CH], F32)
    mA_b = sb("mA_b", [128, CH], F32)
    B_b = sb("B_b", [128, CH], F32)
    tq1 = sb("tq1", [128, CH], F32)
    tq2 = sb("tq2", [128, CH], F32)
    tq3 = sb("tq3", [128, CH], F32)
    tqi = sb("tqi", [128, CH], I32)
    stg_x = sb("stg_x", [128, 2, CH], BF16)
    stg_q = sb("stg_q", [128, 2, CH], BF16)
    brc = sb("brc", [128, CH], BF16)
    rowh = sb("rowh", [1, CH], BF16)
    rowlb = sb("rowlb", [1, CH], BF16)
    colsc = sb("colsc", [128, 8], F32)
    ones_bf = sb("ones_bf", [128, 128], BF16)
    ident_bf = sb("ident_bf", [128, 128], BF16)
    ident_f = sb("ident_f", [128, 128], F32)
    ones_f = sb("ones_f", [128, 128], F32)
    cpool = sb("cpool", [128, 4], F32)
    dum = sb("dum", [128, 32], F32)
    arel = sb("arel_c", [128, 8], F32)
    aimg = sb("aimg_c", [128, 8], F32)
    sp8 = sb("sp8", [128, 8], F32)
    sp_t = sb("sp_t", [128, 8], F32)
    b_in = sb("b_in_c", [128, 16], F32)
    b_out = sb("b_out_c", [128, 16], F32)
    b_gx = sb("b_gx_c", [128, 8], F32)
    b_ga = sb("b_ga_c", [128, 8], F32)
    wgx = sb("wgxs", [128, 16, 128], BF16)
    wga = sb("wgas", [128, 16, 128], BF16)

    pg = [psum("pg0", [128, 512]), psum("pg1", [128, 512])]
    pu = [psum("pu0", [128, 512]), psum("pu1", [128, 512])]
    py = [psum("py0", [128, 512]), psum("py1", [128, 512])]
    pf = psum("pf", [128, 512])
    pm = psum("pm", [128, 512])
    ALLB = pg + pu + py + [pf, pm]
    pm_bf = pm[:].bitcast(BF16)

    sems = {}
    for k in PB.SEMKEYS:
        sems[k] = es.enter_context(nc.semaphore("sem_" + k))

    P = PB()
    pi2 = cpool[:, 0:1]
    c_one = cpool[:, 1:2]
    c_e13 = cpool[:, 2:3]
    c_e5 = cpool[:, 3:4]
    bank_free = {id(b): None for b in ALLB}
    rowf = tq1[0:1, 0:CH]
    rowl = tq2[0:1, 0:CH]

    g0 = P.mile("g", lambda g: (
        g.memset(ones_bf[:], 1.0),
        g.memset(ones_f[:], 1.0),
        g.memset(zeros_f[:], 0.0),
        g.memset(brc[:], 0.0),
        g.memset(thC[:], 0.0),
        g.memset(grC[:], 0.0),
        g.memset(giC[:], 0.0),
        g.memset(pw[:], 0.0),
        g.memset(pr[:], 0.0),
        g.memset(dum[:], 1.0),
        g.memset(xq[:], 0.0),
        g.memset(cpool[:, 0:1], float(np.pi / 2)),
        g.memset(cpool[:, 1:2], 1.0),
        g.memset(cpool[:, 2:3], 1e-13),
        g.memset(cpool[:, 3:4], 1e-5),
        g.affine_select(ident_bf[:], ones_bf[:], pattern=[[-1, 128]],
                        compare_op=ALU.is_equal, fill=0.0, base=0,
                        channel_multiplier=1),
        g.affine_select(ident_f[:], ones_f[:], pattern=[[-1, 128]],
                        compare_op=ALU.is_equal, fill=0.0, base=0,
                        channel_multiplier=1),
    ) and None)

    for e_ in ("t", "v", "s", "q"):
        P.need(e_, g0)
    ctoks = {}
    for nm, dst, src in (("arel", arel, arel_in), ("aimg", aimg, aimg_in),
                         ("b_in", b_in, bin_in), ("b_out", b_out, bout_in),
                         ("b_gx", b_gx, bgx_in), ("b_ga", b_ga, bga_in)):
        ctoks[nm] = P.dma("q", lambda q, d=dst, s_=src: q.dma_start(out=d[:], in_=s_))
    pwf = pw[:].bitcast(F32).rearrange("p a (k t) -> p (a k) t", t=128)
    prf = pr[:].bitcast(F32).rearrange("p a (k t) -> p (a k) t", t=128)
    t1_ = P.dma("q", lambda q: q.dma_start(out=pwf[:, 0:8], in_=wgx_in[:, 0:8]))
    t2_ = P.dma("q", lambda q: q.dma_start(out=prf[:, 0:8], in_=wgx_in[:, 8:16]))
    P.need("v", t1_)
    P.need("v", t2_)
    cast_gx = P.mile("v", lambda v: (
        v.tensor_copy(wgx[:, 0:8], pwf[:, 0:8]),
        v.tensor_copy(wgx[:, 8:16], prf[:, 0:8]),
    ) and None)
    P.need("q", cast_gx)
    t3_ = P.dma("q", lambda q: q.dma_start(out=pwf[:, 0:8], in_=wga_in[:, 0:8]))
    t4_ = P.dma("q", lambda q: q.dma_start(out=prf[:, 0:8], in_=wga_in[:, 8:16]))
    P.need("v", t3_)
    P.need("v", t4_)
    cast_ga = P.mile("v", lambda v: (
        v.tensor_copy(wga[:, 0:8], pwf[:, 0:8]),
        v.tensor_copy(wga[:, 8:16], prf[:, 0:8]),
    ) and None)
    pwpr_free = cast_ga

    for nm_ in ("arel", "aimg", "b_in", "b_out", "b_gx", "b_ga"):
        P.need("s", ctoks[nm_])
        P.need("v", ctoks[nm_])
    P.need("s", g0)
    sp_tok = P.mile("s", lambda s: (
        s.activation(sp_t[:], arel[:], AF.Abs),
        s.drain(),
        s.activation(sp_t[:], sp_t[:], AF.Exp, scale=-1.0),
        s.drain(),
        s.activation(sp_t[:], sp_t[:], AF.Ln, bias=c_one, scale=1.0),
    ) and None)
    P.need("v", sp_tok)
    sp8_tok = P.mile("v", lambda v: (
        v.tensor_relu(sp8[:], arel[:]),
        v.drain(),
        v.tensor_add(sp8[:], sp8[:], sp_t[:]),
        v.drain(),
        v.tensor_scalar_mul(sp8[:], sp8[:], -8.0),
    ) and None)

    def xslice(c):
        return x_fm.rearrange("(k p) t -> p k t", p=128)[:, :, c * CH:(c + 1) * CH]

    tok_x0 = P.dma("q", lambda q: q.dma_start(out=slot[:, 0], in_=xslice(0)))

    GUARD = {"trA": None, "trB": None, "trC": None, "bc": None, "stat": None,
             "wout": None, "xhmm": None, "gatemm": None, "scanv": None,
             "prepS": None, "prepT": None, "prdn": None, "prq": None}

    def emit_quant(slot_ap, out_ap, mode, dep):
        P.need("t", bank_free[id(pf)])
        P.need("t", bank_free[id(pm)])
        tk_prev = [GUARD["stat"], GUARD["stat"]]
        stat_tok = None
        for k in range(KD):
            P.need("v", dep)
            P.need("v", tk_prev[k % 2])
            ct = P.mile("v", lambda v, k=k: v.tensor_copy(
                stg_x[:, k % 2], slot_ap[:, k]))
            P.need("s", dep)
            P.need("s", tk_prev[k % 2])
            sq = P.mile("s", lambda s, k=k: s.activation(
                stg_q[:, k % 2], slot_ap[:, k], AF.Square))
            P.need("t", ct)
            P.need("t", sq)
            P.run("t", lambda t, k=k: t.matmul(
                pf[:, 0:CH], ones_bf[:], stg_x[:, k % 2],
                start=(k == 0), stop=(k == KD - 1)))
            stat_tok = P.mile("t", lambda t, k=k: t.matmul(
                pm[:, 0:CH], ones_bf[:], stg_q[:, k % 2],
                start=(k == 0), stop=(k == KD - 1)))
            tk_prev[k % 2] = stat_tok
        GUARD["stat"] = stat_tok
        P.need("v", stat_tok)
        P.need("v", GUARD["trB"])
        m_tok = P.mile("v", lambda v: (
            spin_v(v),
            v.tensor_scalar_mul(m_b[:], pf[:, 0:CH], 1.0 / D),
        ) and None)
        bank_free[id(pf)] = m_tok
        P.need("s", m_tok)
        msq_tok = P.mile("s", lambda s: s.activation(tq3[:], m_b[:], AF.Square))
        P.need("v", msq_tok)
        v_tok = P.mile("v", lambda v: (
            spin_v(v),
            v.scalar_tensor_tensor(
                v_b[:], pm[:, 0:CH], 1.0 / D, tq3[:], ALU.mult, ALU.subtract),
        ) and None)
        bank_free[id(pm)] = v_tok
        P.need("s", v_tok)
        if mode == "ffn":
            r_tok = P.mile("s", lambda s: (
                s.activation(R_b[:], v_b[:], AF.Ln, bias=c_e13, scale=1.00000001),
                s.drain(),
                s.activation(R_b[:], R_b[:], AF.Exp, scale=-0.5),
            ) and None)
        else:
            r_tok = P.mile("s", lambda s: (
                s.activation(R_b[:], v_b[:], AF.Ln, bias=c_e5, scale=1.0),
                s.drain(),
                s.activation(R_b[:], R_b[:], AF.Exp, scale=-0.5),
            ) and None)

        if mode == "ln2":
            P.need("v", r_tok)
            P.need("v", GUARD["xhmm"])
            P.mile("v", lambda v: v.tensor_mul(mA_b[:], m_b[:], R_b[:]))
            last = None
            for k in range(KD):
                last = P.mile("v", lambda v, k=k: (
                    v.tensor_mul(tq1[:], slot_ap[:, k], R_b[:]),
                    v.tensor_sub(out_ap[:, k], tq1[:], mA_b[:]),
                ) and None)
            return last

        P.need("v", dep)
        P.need("v", GUARD["trA"])
        tmm_tok = P.mile("v", lambda v: (
            v.tensor_copy(Mx[:], slot_ap[:, 0]),
            v.tensor_copy(Mn[:], slot_ap[:, 0]),
            [(v.tensor_max(Mx[:], Mx[:], slot_ap[:, k]),
              v.tensor_tensor(Mn[:], Mn[:], slot_ap[:, k], ALU.min))
             for k in range(1, KD)],
        ) and None)
        P.need("t", bank_free[id(pf)])
        P.need("t", tmm_tok)
        trA = P.mile("t", lambda t: (
            t.transpose(pf[:, 0:128], Mx[:, 0:128], ident_f[:]),
            t.transpose(pf[:, 128:256], Mx[:, 128:256], ident_f[:]),
            t.transpose(pf[:, 256:384], Mn[:, 0:128], ident_f[:]),
            t.transpose(pf[:, 384:512], Mn[:, 128:256], ident_f[:]),
        )[-1])
        GUARD["trA"] = trA
        P.need("t", bank_free[id(pm)])
        trB = P.mile("t", lambda t: (
            t.transpose(pm[:, 0:128], m_b[:, 0:128], ident_f[:]),
            t.transpose(pm[:, 128:256], m_b[:, 128:256], ident_f[:]),
        )[-1])
        GUARD["trB"] = trB
        P.need("v", trA)
        P.need("v", trB)
        P.need("v", GUARD["trC"])
        mx0_tok = P.mile("v", lambda v: (
            spin_v(v),
            v.reduce_max(colsc[:, 0:1], pf[:, 0:128], axis=AX.X),
            v.reduce_max(colsc[:, 1:2], pf[:, 128:256], axis=AX.X),
            v.tensor_reduce(colsc[:, 2:3], pf[:, 256:384], axis=AX.X, op=ALU.min),
            v.tensor_reduce(colsc[:, 3:4], pf[:, 384:512], axis=AX.X, op=ALU.min),
            v.tensor_copy(colsc[:, 4:5], pm[:, 0:1]),
            v.tensor_copy(colsc[:, 5:6], pm[:, 128:129]),
            v.drain(),
            v.tensor_sub(colsc[:, 6:7], colsc[:, 0:1], colsc[:, 4:5]),
            v.tensor_sub(colsc[:, 7:8], colsc[:, 4:5], colsc[:, 2:3]),
            v.drain(),
            v.tensor_max(colsc[:, 6:7], colsc[:, 6:7], colsc[:, 7:8]),
            v.drain(),
            v.tensor_sub(colsc[:, 7:8], colsc[:, 1:2], colsc[:, 5:6]),
            v.tensor_sub(colsc[:, 0:1], colsc[:, 5:6], colsc[:, 3:4]),
            v.drain(),
            v.tensor_max(colsc[:, 7:8], colsc[:, 7:8], colsc[:, 0:1]),
        ) and None)
        bank_free[id(pf)] = mx0_tok
        bank_free[id(pm)] = mx0_tok
        P.need("t", mx0_tok)
        trC = P.mile("t", lambda t: (
            t.transpose(pf[0:1, 0:128], colsc[:, 6:7], ident_f[:]),
            t.transpose(pf[0:1, 128:256], colsc[:, 7:8], ident_f[:]),
        )[-1])
        GUARD["trC"] = trC
        P.need("v", trC)
        P.need("v", GUARD["bc"])
        brc_tok = P.mile("v", lambda v: (
            spin_v(v),
            v.tensor_copy(rowf, pf[0:1, 0:CH]),
            v.drain(),
            v.tensor_copy(rowh[:], rowf),
            v.drain(),
            v.tensor_sub(rowl, rowf, rowh[:]),
            v.drain(),
            v.tensor_copy(rowlb[:], rowl),
            v.drain(),
            v.tensor_copy(brc[0:1, :], rowh[:]),
            v.tensor_copy(brc[32:33, :], rowlb[:]),
        ) and None)
        bank_free[id(pf)] = brc_tok
        P.need("t", brc_tok)
        bc_tok = P.mile("t", lambda t: t.matmul(
            pm[:, 0:CH], ones_bf[:], brc[:], start=True, stop=True))
        GUARD["bc"] = bc_tok
        P.need("v", bc_tok)
        P.need("v", r_tok)
        ab_tok = P.mile("v", lambda v: (
            spin_v(v),
            v.reciprocal(tq3[:], pm[:, 0:CH]),
            v.scalar_tensor_tensor(B_b[:], pm[:, 0:CH], 1.0 / 127.0, R_b[:],
                                   ALU.mult, ALU.mult),
            v.drain(),
            v.tensor_scalar_mul(A_b[:], tq3[:], 127.0),
            v.drain(),
            v.tensor_mul(mA_b[:], m_b[:], A_b[:]),
        ) and None)
        bank_free[id(pm)] = ab_tok
        last = None
        for k in range(KD):
            last = P.mile("v", lambda v, k=k: (
                v.tensor_mul(tq1[:], slot_ap[:, k], A_b[:]),
                v.tensor_sub(tq1[:], tq1[:], mA_b[:]),
                v.tensor_copy(tqi[:], tq1[:]),
                v.tensor_copy(tq2[:], tqi[:]),
                v.tensor_sub(tq1[:], tq1[:], tq2[:]),
                v.tensor_single_scalar(tq3[:], tq1[:], 0.5, ALU.is_gt),
                v.tensor_single_scalar(tq1[:], tq1[:], -0.5, ALU.is_lt),
                v.tensor_sub(tq1[:], tq1[:], tq3[:]),
                v.tensor_sub(tq2[:], tq2[:], tq1[:]),
                v.tensor_mul(out_ap[:, k], tq2[:], B_b[:]),
            ) and None)
        return last

    def spin_s(s):
        s.activation(dum[:, 0:16], dum[:, 0:16], AF.Abs)

    def spin_v(v):
        v.tensor_scalar_mul(dum[:, 16:32], dum[:, 16:32], 1.0)

    prep_state = {"par": 0, "guard": [None, None]}

    def emit_prep_chunk(src_ap, dst_slab, ring_guards):
        par = prep_state["par"]
        prep_state["par"] ^= 1
        gtok = prep_state["guard"][par]
        P.need("q", gtok)
        dtok = P.dma("q", lambda q, par=par, src_ap=src_ap: q.dma_start(
            out=prep_f[par], in_=src_ap))
        P.need("s", dtok)
        sg_tok = P.mile("s", lambda s, par=par: s.activation(
            prep_b[par], prep_f[par], AF.Sign))
        GUARD["prepS"] = sg_tok
        ev = None
        for rnd in range(2):
            P.need("t", bank_free[id(pm)])
            P.need("t", sg_tok)
            ttok = P.mile("t", lambda t, par=par, rnd=rnd: [
                t.transpose(pm_bf[:, j * 128:(j + 1) * 128],
                            prep_b[par][:, (rnd * 8 + j) * 128:(rnd * 8 + j + 1) * 128],
                            ident_bf[:])
                for j in range(8)
            ][-1])
            GUARD["prepT"] = ttok
            P.need("v", ttok)
            for rg in ring_guards:
                P.need("v", rg)
            ev = P.mile("v", lambda v, rnd=rnd, dst_slab=dst_slab: v.tensor_copy(
                dst_slab[:, rnd * 8:(rnd + 1) * 8],
                pm_bf[:].rearrange("p (j t) -> p j t", t=128)))
            bank_free[id(pm)] = ev
        prep_state["guard"][par] = ev
        return ev

    xq_tok = [None, None]
    xq_free = [None, None]
    slot_store = [None] * NCHUNK
    slot_ready = [None] * NCHUNK
    ring_tok = {}
    scr_tok = {}
    pwr_tok = [None] * (MH // 4)
    io_prep = {"f": [None, None]}

    slot_ready[0] = tok_x0
    q0 = emit_quant(slot[:, 0], xq[:, 0, :, 0:CH], "ffn", tok_x0)
    xq_tok[0] = q0
    if dbg:
        P.need("s", q0)
        P.dma("s", lambda se: se.dma_start(out=dbg_t["dbg_xq"], in_=xq[:, 0, :, 0:CH]))

    for s_ in range(NSS):
        A = s_ if s_ < NCHUNK else None
        C = s_ - 2 if s_ >= 2 else None
        par = s_ % 2
        c0, c1 = (0, 512) if (A is not None and C is not None) else \
                 ((0, CH) if A is not None else (CH, 512))

        if dbg and s_ == 8:
            P.need("s", xq_tok[par])
            P.dma("s", lambda se: se.dma_start(out=dbg_t["dbg_r2c6"],
                                               in_=slot[:, 0]))
            P.dma("s", lambda se: se.dma_start(out=dbg_t["dbg_xqc6"],
                                               in_=xq[:, 0, :, CH:512]))

        if s_ == 1:
            for key in (("g", 0), ("g", 1), ("u", 0), ("u", 1),
                        ("d0",), ("d1",), ("io", 0), ("io", 1)):
                P.need("q", scr_tok.get(key))
        for m in range(MH):
            mp = m % 2
            if s_ == 0:
                wg_tok = emit_prep_chunk(
                    w_g.rearrange("(m p) d -> m p d", p=128)[m], wgT[:, mp],
                    [ring_tok.get(("g", mp)), scr_tok.get(("g", mp))])
                P.need("g", wg_tok)
                scr_tok[("g", mp)] = P.dma("g", lambda g, m=m, mp=mp: g.dma_start(
                    out=wgT_s[m], in_=wgT[:, mp]))
                wu_tok = emit_prep_chunk(
                    w_u.rearrange("(m p) d -> m p d", p=128)[m], wuT[:, mp],
                    [ring_tok.get(("u", mp)), scr_tok.get(("u", mp))])
                P.need("g", wu_tok)
                scr_tok[("u", mp)] = P.dma("g", lambda g, m=m, mp=mp: g.dma_start(
                    out=wuT_s[m], in_=wuT[:, mp]))
            else:
                P.need("q", ring_tok.get(("g", mp)))
                wg_tok = P.dma("q", lambda q, m=m, mp=mp: q.dma_start(
                    out=wgT[:, mp], in_=wgT_s[m]))
                P.need("q", ring_tok.get(("u", mp)))
                wu_tok = P.dma("q", lambda q, m=m, mp=mp: q.dma_start(
                    out=wuT[:, mp], in_=wuT_s[m]))
            if dbg and s_ == 8 and m == 0:
                P.need("g", wg_tok)
                P.dma("g", lambda g: g.dma_start(
                    out=dbg_t["dbg_wg8"], in_=wgT[:, 0]))
            P.need("t", wg_tok)
            P.need("t", xq_tok[par])
            P.need("t", bank_free[id(pg[mp])])
            gmm = P.mile("t", lambda t, mp=mp, par=par: [
                t.matmul(pg[mp][:], wgT[:, mp, k], xq[:, par, k],
                         start=(k == 0), stop=(k == KD - 1))
                for k in range(KD)
            ][-1])
            ring_tok[("g", mp)] = gmm
            P.need("t", wu_tok)
            P.need("t", bank_free[id(pu[mp])])
            umm = P.mile("t", lambda t, mp=mp, par=par: [
                t.matmul(pu[mp][:], wuT[:, mp, k], xq[:, par, k],
                         start=(k == 0), stop=(k == KD - 1))
                for k in range(KD)
            ][-1])
            ring_tok[("u", mp)] = umm
            P.need("s", gmm)
            P.need("s", ring_tok.get(("gb", mp)))
            sil = P.mile("s", lambda s, mp=mp: (
                spin_s(s),
                s.activation(gbuf[:, mp], pg[mp][:], AF.Silu),
            ) and None)
            bank_free[id(pg[mp])] = sil
            P.need("v", umm)
            P.need("v", sil)
            if s_ > 0 or m >= 4:
                P.need("v", pwr_tok[(m - 4) // 4])
            else:
                P.need("v", pwpr_free)
            if s_ == 1 and m < 4:
                P.need("v", GUARD["prq"])
            ptk = P.mile("v", lambda v, m=m, mp=mp: (
                spin_v(v),
                v.scalar_tensor_tensor(
                    pw[:, m % 4], pu[mp][:], 0.5,
                    gbuf[:, mp], ALU.mult, ALU.mult),
            ) and None)
            bank_free[id(pu[mp])] = ptk
            ring_tok[("gb", mp)] = ptk
            if dbg and s_ == 8 and m == 0:
                P.need("g", ptk)
                P.dma("g", lambda g: g.dma_start(
                    out=dbg_t["dbg_pw8"], in_=pw[:, 0, CH:512]))
                P.dma("g", lambda g: g.dma_start(
                    out=dbg_t["dbg_gb8"], in_=gbuf[:, 0, CH:512]))
            if m % 4 == 3:
                P.need("g", ptk)
                pwr_tok[m // 4] = P.dma("g", lambda g, m=m, par=par: g.dma_start(
                    out=p_scr[par, m - 3:m + 1].rearrange("r p n -> p r n"),
                    in_=pw[:]))

        if dbg and s_ == 0:
            P.need("g", pwr_tok[-1])
            P.dma("g", lambda g: g.dma_start(
                out=dbg_t["dbg_p"], in_=p_scr[0, :, :, 0:CH]))

        if dbg and s_ in (8, 9):
            P.need("g", pwr_tok[-1])
            P.dma("g", lambda g, s_=s_, par=par: g.dma_start(
                out=dbg_t["dbg_p9"][s_ - 8], in_=p_scr[par, :, :, CH:512]))
        if dbg and s_ == 9:
            P.need("s", xq_tok[par])
            P.dma("s", lambda se, par=par: se.dma_start(
                out=dbg_t["dbg_xqc7"], in_=xq[:, par, :, CH:512]))

        if s_ == 0:
            for m in range(MH):
                wd_ready = emit_prep_chunk(
                    w_d.rearrange("(dk p) hh -> p dk hh", p=128)
                    [:, :, m * 128:(m + 1) * 128],
                    wdprep,
                    [scr_tok.get(("d0",)), scr_tok.get(("d1",))])
                P.need("g", wd_ready)
                scr_tok[("d0",)] = P.dma("g", lambda g, m=m: g.dma_start(
                    out=wdT_s[0, m], in_=wdprep[:, 0:8]))
                scr_tok[("d1",)] = P.dma("g", lambda g, m=m: g.dma_start(
                    out=wdT_s[1, m], in_=wdprep[:, 8:16]))

        ev_last = None
        for half in range(2):
            for b in ALLB:
                P.need("t", bank_free[id(b)])
            dn_tok = [None] * MH
            pr_dma = [None] * (MH // 4)
            for m in range(MH):
                mp = m % 2
                if True:
                    P.need("q", ring_tok.get(("d", mp)))
                    if s_ == 0:
                        P.need("q", scr_tok[("d0",)])
                        P.need("q", scr_tok[("d1",)])
                    wd_tok = P.dma("q", lambda q, m=m, mp=mp, half=half:
                                   q.dma_start(out=wdT[:, mp], in_=wdT_s[half, m]))
                    P.need("t", wd_tok)
                    wd_src = wdT[:, mp]
                if m % 4 == 0:
                    P.need("g", pwr_tok[m // 4])
                    P.need("g", dn_tok[m - 1] if m >= 1 else GUARD["prdn"])
                    if s_ == 1 and m == 0:
                        P.need("g", GUARD["prq"])
                    pr_dma[m // 4] = P.dma("g", lambda g, m=m, par=par: g.dma_start(
                        out=pr[:],
                        in_=p_scr[par, m:m + 4].rearrange("r p n -> p r n")))
                if dbg and s_ == 8 and half == 0 and m == 0:
                    P.need("g", pr_dma[0])
                    P.dma("g", lambda g: g.dma_start(
                        out=dbg_t["dbg_pr8"], in_=pr[:, :, CH:512]))
                    P.dma("g", lambda g: g.dma_start(
                        out=dbg_t["dbg_p8"],
                        in_=p_scr[0, 0:4, :, CH:512]))
                if dbg and s_ == 0 and half == 0 and m == MH - 1:
                    P.need("g", pr_dma[m // 4])
                    P.need("g", wd_tok)
                    P.dma("g", lambda g: g.dma_start(
                        out=dbg_t["dbg_pr"], in_=pr[:, :, 0:CH]))
                    P.dma("g", lambda g: g.dma_start(
                        out=dbg_t["dbg_wd"], in_=wdT[:]))
                P.need("t", pr_dma[m // 4])
                dn_tok[m] = P.mile("t", lambda t, m=m, wd_src=wd_src: [
                    t.matmul(ALLB[d][:], wd_src[:, d], pr[:, m % 4],
                             start=(m == 0), stop=(m == MH - 1))
                    for d in range(8)
                ][-1])
                ring_tok[("d", mp)] = dn_tok[m]
            GUARD["prdn"] = dn_tok[MH - 1]
            for d8 in range(8):
                d = half * 8 + d8
                bank = ALLB[d8]
                P.need("v", dn_tok[MH - 1])
                if A is not None:
                    P.need("v", slot_ready[A])
                if C is not None:
                    P.need("v", slot_ready[C])

                def ev_fn(v, d=d, bank=bank, A=A, C=C):
                    spin_v(v)
                    r = None
                    if A is not None:
                        r = v.tensor_add(slot[:, A % 3, d], slot[:, A % 3, d],
                                         bank[:, 0:CH])
                    if C is not None:
                        r = v.tensor_add(slot[:, C % 3, d], slot[:, C % 3, d],
                                         bank[:, CH:512])
                    return r
                evt = P.mile("v", ev_fn)
                bank_free[id(bank)] = evt
                ev_last = evt
                if C is not None:
                    P.need("s", evt)
                    stok = P.dma("s", lambda se, d=d, C=C: se.dma_start(
                        out=y_fm.rearrange("(k p) t -> k p t", p=128)
                        [d][:, C * CH:(C + 1) * CH],
                        in_=slot[:, C % 3, d]))
                    if d == 15:
                        slot_store[C] = stok
        if A is not None:
            slot_ready[A] = ev_last
        xq_free[par] = ev_last

        if A is None:
            continue
        chunk = A
        sl = slot[:, chunk % 3]
        if dbg and s_ == 0:
            P.need("s", ev_last)
            P.dma("s", lambda se, sl=sl: se.dma_start(out=dbg_t["dbg_r1"], in_=sl))
        dump_tok = None
        if dbg:
            P.need("s", ev_last)
            dump_tok = P.dma("s", lambda se, sl=sl, chunk=chunk: se.dma_start(
                out=dbg_t["dbg_r1a"][chunk], in_=sl))

        if s_ == 0:
            P.need("v", GUARD["prepS"])
        ln2_tok = emit_quant(sl, l2b, "ln2", slot_ready[chunk])
        xh_toks = []
        for Mt in range(KD):
            mp = Mt % 2
            if s_ == 0:
                P.need("q", pwr_tok[-1])
                P.need("q", GUARD["prdn"])
                P.need("q", io_prep["f"][0])
                d1 = P.dma("q", lambda q, Mt=Mt: q.dma_start(
                    out=pw[:].bitcast(F32).rearrange("p a (k t) -> p (a k) t", t=128),
                    in_=w_in.rearrange("(k p) o -> p k o", p=128)
                    [:, 0:8, Mt * 128:(Mt + 1) * 128]))
                P.need("q", io_prep["f"][1])
                d2 = P.dma("q", lambda q, Mt=Mt: q.dma_start(
                    out=pr[:].bitcast(F32).rearrange("p a (k t) -> p (a k) t", t=128),
                    in_=w_in.rearrange("(k p) o -> p k o", p=128)
                    [:, 8:16, Mt * 128:(Mt + 1) * 128]))
                GUARD["prq"] = d2
                P.need("v", d1)
                P.need("v", d2)
                P.need("v", ring_tok.get(("io", mp)))
                P.need("v", scr_tok.get(("io", mp)))
                io_tok = P.mile("v", lambda v, mp=mp: (
                    v.tensor_copy(wio[:, mp, 0:8],
                                  pw[:].bitcast(F32).rearrange(
                                      "p a (k t) -> p (a k) t", t=128)),
                    v.tensor_copy(wio[:, mp, 8:16],
                                  pr[:].bitcast(F32).rearrange(
                                      "p a (k t) -> p (a k) t", t=128)),
                ) and None)
                io_prep["f"][0] = io_tok
                io_prep["f"][1] = io_tok
                P.need("g", io_tok)
                scr_tok[("io", mp)] = P.dma("g", lambda g, Mt=Mt, mp=mp:
                                            g.dma_start(out=winB_s[Mt],
                                                        in_=wio[:, mp]))
            else:
                P.need("q", ring_tok.get(("io", mp)))
                io_tok = P.dma("q", lambda q, Mt=Mt, mp=mp: q.dma_start(
                    out=wio[:, mp], in_=winB_s[Mt]))
            P.need("t", io_tok)
            P.need("t", ln2_tok)
            P.need("t", bank_free[id(pg[mp])])
            mmt = P.mile("t", lambda t, mp=mp: [
                t.matmul(pg[mp][:, 0:CH], wio[:, mp, k], l2b[:, k],
                         start=(k == 0), stop=(k == KD - 1))
                for k in range(KD)
            ][-1])
            ring_tok[("io", mp)] = mmt
            GUARD["xhmm"] = mmt
            P.need("s", mmt)
            P.need("s", GUARD["gatemm"])
            P.need("s", GUARD["scanv"])
            P.need("s", GUARD["prepT"])
            xh_tok = P.mile("s", lambda s, Mt=Mt, mp=mp: (
                spin_s(s),
                s.activation(xh[:, Mt], pg[mp][:, 0:CH], AF.Identity,
                             bias=b_in[:, Mt:Mt + 1]),
            ) and None)
            bank_free[id(pg[mp])] = xh_tok
            xh_toks.append(xh_tok)
        gate_last = None
        for h in range(8):
            for gi_, wt, bt, ot, ctok in ((0, wgx, b_gx, gx_t, cast_gx),
                                          (1, wga, b_ga, ga_t, cast_ga)):
                mp = (h * 2 + gi_) % 2
                P.need("t", xh_toks[2 * h])
                P.need("t", xh_toks[2 * h + 1])
                P.need("t", ctok)
                P.need("t", bank_free[id(pu[mp])])
                gmt = P.mile("t", lambda t, h=h, wt=wt, mp=mp: (
                    t.matmul(pu[mp][:, 0:CH], wt[:, 2 * h], xh[:, 2 * h],
                             start=True, stop=False),
                    t.matmul(pu[mp][:, 0:CH], wt[:, 2 * h + 1], xh[:, 2 * h + 1],
                             start=False, stop=True),
                )[-1])
                GUARD["gatemm"] = gmt
                P.need("s", gmt)
                gate_last = P.mile("s", lambda s, h=h, bt=bt, ot=ot, mp=mp: (
                    spin_s(s),
                    s.activation(ot[:, h], pu[mp][:, 0:CH],
                                 AF.Sigmoid, bias=bt[:, h:h + 1]),
                ) and None)
                bank_free[id(pu[mp])] = gate_last

        if dbg and s_ == 0:
            P.need("s", ln2_tok)
            P.dma("s", lambda se: se.dma_start(out=dbg_t["dbg_l2b"], in_=l2b[:]))
            P.need("s", xh_toks[-1])
            P.dma("s", lambda se: se.dma_start(out=dbg_t["dbg_xh"], in_=xh[:]))
            P.need("s", gate_last)
            P.dma("s", lambda se: se.dma_start(out=dbg_t["dbg_gx"], in_=gx_t[:]))
            P.dma("s", lambda se: se.dma_start(out=dbg_t["dbg_ga"], in_=ga_t[:]))

        t1, t2, t3, t4, t5, t6, t7 = st_
        hm_tok = None
        for sub in range(2):
            o_ = sub * 128
            gxs = gx_t[:, :, o_:o_ + 128]
            gas = ga_t[:, :, o_:o_ + 128]
            xrs = xh[:, 0:8, o_:o_ + 128]
            xis = xh[:, 8:16, o_:o_ + 128]
            P.need("v", gate_last)
            P.need("v", xh_toks[-1])
            P.need("v", sp8_tok)
            P.need("v", ctoks["aimg"])
            v1 = P.mile("v", lambda v, gas=gas: [
                (v.tensor_scalar_mul(t1[:, c], gas[:, c], sp8[:, c:c + 1]),
                 v.tensor_scalar_mul(t3[:, c], gas[:, c], aimg[:, c:c + 1]))
                for c in range(8)
            ] and None)
            P.need("s", v1)
            s1 = P.mile("s", lambda s: (
                s.activation(t2[:], t1[:], AF.Exp),
                s.activation(t4[:], t3[:], AF.Sin, bias=pi2, scale=-2.0),
                s.activation(t5[:], t3[:], AF.Sin, scale=2.0),
                s.drain(),
                s.activation(t1[:], t1[:], AF.Exp, scale=2.0),
            ) and None)
            P.need("v", s1)
            v2 = P.mile("v", lambda v: (
                [v.tensor_tensor_scan(t6[:, c], zeros_f[:], t3[:, c],
                                      thC[:, c:c + 1], ALU.add, ALU.add)
                 for c in range(8)],
                v.drain(),
                v.tensor_copy(thC[:], t6[:, :, 127]),
                v.drain(),
                v.tensor_mul(t4[:], t1[:], t4[:]),
                v.tensor_mul(t5[:], t1[:], t5[:]),
                v.drain(),
                v.tensor_scalar(t4[:], t4[:], -1.0, 1.0, ALU.mult, ALU.add),
            ) and None)
            P.need("s", v2)
            s2 = P.mile("s", lambda s: (
                s.activation(t3[:], t4[:], AF.Square),
                s.activation(t1[:], t5[:], AF.Square),
            ) and None)
            P.need("v", s2)
            v3 = P.mile("v", lambda v: v.tensor_add(t3[:], t3[:], t1[:]))
            P.need("s", v3)
            s3 = P.mile("s", lambda s: s.activation(t3[:], t3[:], AF.Sqrt))
            P.need("v", s3)
            v4 = P.mile("v", lambda v: v.tensor_add(t3[:], t3[:], t4[:]))
            P.need("s", v4)
            s4 = P.mile("s", lambda s: (
                s.activation(t4[:], t3[:], AF.Sqrt, scale=0.5),
                s.activation(t3[:], t3[:], AF.Sqrt, scale=2.0),
            ) and None)
            P.need("v", s4)
            v5 = P.mile("v", lambda v, gxs=gxs, xrs=xrs, xis=xis: (
                v.reciprocal(t1[:], t3[:]),
                v.drain(),
                v.tensor_mul(t5[:], t5[:], t1[:]),
                v.drain(),
                v.tensor_mul(t4[:], t4[:], gxs[:]),
                v.tensor_mul(t5[:], t5[:], gxs[:]),
                v.drain(),
                v.tensor_mul(t1[:], t4[:], xrs[:]),
                v.tensor_mul(t7[:], t5[:], xis[:]),
                v.drain(),
                v.tensor_add(t1[:], t1[:], t7[:]),
                v.tensor_mul(t7[:], t4[:], xis[:]),
                v.tensor_mul(t3[:], t5[:], xrs[:]),
                v.drain(),
                v.tensor_sub(t7[:], t7[:], t3[:]),
                v.tensor_scalar_mul(t3[:], t6[:], INV_2PI),
                v.drain(),
                v.tensor_copy(sti, t3[:]),
                v.drain(),
                v.tensor_copy(t3[:], sti),
                v.drain(),
                v.scalar_tensor_tensor(t4[:], t3[:], TWO_PI_HI, t6[:],
                                       ALU.mult, ALU.subtract),
                v.drain(),
                v.scalar_tensor_tensor(t4[:], t3[:], TWO_PI_LO, t4[:],
                                       ALU.mult, ALU.add),
                v.drain(),
                v.tensor_single_scalar(t3[:], t4[:], -float(np.pi), ALU.is_lt),
                v.drain(),
                v.scalar_tensor_tensor(t4[:], t3[:], TWO_PI_HI, t4[:],
                                       ALU.mult, ALU.add),
                v.scalar_tensor_tensor(t4[:], t3[:], TWO_PI_LO, t4[:],
                                       ALU.mult, ALU.add),
            ) and None)
            P.need("s", v5)
            s5 = P.mile("s", lambda s: (
                s.activation(t5[:], t4[:], AF.Abs),
                s.activation(t3[:], t4[:], AF.Sin),
                s.drain(),
                s.activation(t5[:], t5[:], AF.Sin, bias=pi2, scale=-1.0),
            ) and None)
            P.need("v", s5)
            P.need("v", GUARD["wout"])
            hm_tok = P.mile("v", lambda v, o_=o_: (
                v.tensor_mul(t4[:], t5[:], t1[:]),
                v.tensor_mul(t6[:], t3[:], t7[:]),
                v.drain(),
                v.tensor_sub(t4[:], t4[:], t6[:]),
                v.tensor_mul(t6[:], t5[:], t7[:]),
                v.drain(),
                v.tensor_mul(t7[:], t3[:], t1[:]),
                v.drain(),
                v.tensor_add(t6[:], t6[:], t7[:]),
                v.drain(),
                [v.tensor_tensor_scan(t1[:, c], t2[:, c], t4[:, c],
                                      grC[:, c:c + 1], ALU.mult, ALU.add)
                 for c in range(8)],
                [v.tensor_tensor_scan(t7[:, c], t2[:, c], t6[:, c],
                                      giC[:, c:c + 1], ALU.mult, ALU.add)
                 for c in range(8)],
                v.drain(),
                v.tensor_copy(grC[:], t1[:, :, 127]),
                v.tensor_copy(giC[:], t7[:, :, 127]),
                v.drain(),
                v.tensor_mul(t4[:], t5[:], t1[:]),
                v.tensor_mul(t6[:], t3[:], t7[:]),
                v.drain(),
                v.tensor_add(hm[:, 0:8, o_:o_ + 128], t4[:], t6[:]),
                v.tensor_mul(t4[:], t5[:], t7[:]),
                v.tensor_mul(t6[:], t3[:], t1[:]),
                v.drain(),
                v.tensor_sub(hm[:, 8:16, o_:o_ + 128], t4[:], t6[:]),
            ) and None)
            GUARD["scanv"] = hm_tok

        r2_tok = None
        for Mt in range(KD):
            mp = Mt % 2
            if s_ == 0:
                P.need("q", io_prep["f"][0])
                d1 = P.dma("q", lambda q, Mt=Mt: q.dma_start(
                    out=pw[:].bitcast(F32).rearrange("p a (k t) -> p (a k) t", t=128),
                    in_=w_out.rearrange("(k p) o -> p k o", p=128)
                    [:, 0:8, Mt * 128:(Mt + 1) * 128]))
                d2 = P.dma("q", lambda q, Mt=Mt: q.dma_start(
                    out=pr[:].bitcast(F32).rearrange("p a (k t) -> p (a k) t", t=128),
                    in_=w_out.rearrange("(k p) o -> p k o", p=128)
                    [:, 8:16, Mt * 128:(Mt + 1) * 128]))
                GUARD["prq"] = d2
                P.need("v", d1)
                P.need("v", d2)
                P.need("v", ring_tok.get(("io", mp)))
                P.need("v", scr_tok.get(("io", mp)))
                io_tok = P.mile("v", lambda v, mp=mp: (
                    v.tensor_copy(wio[:, mp, 0:8],
                                  pw[:].bitcast(F32).rearrange(
                                      "p a (k t) -> p (a k) t", t=128)),
                    v.tensor_copy(wio[:, mp, 8:16],
                                  pr[:].bitcast(F32).rearrange(
                                      "p a (k t) -> p (a k) t", t=128)),
                ) and None)
                io_prep["f"][0] = io_tok
                io_prep["f"][1] = io_tok
                P.need("g", io_tok)
                scr_tok[("io", mp)] = P.dma("g", lambda g, Mt=Mt, mp=mp:
                                            g.dma_start(out=woutB_s[Mt],
                                                        in_=wio[:, mp]))
            else:
                P.need("q", ring_tok.get(("io", mp)))
                io_tok = P.dma("q", lambda q, Mt=Mt, mp=mp: q.dma_start(
                    out=wio[:, mp], in_=woutB_s[Mt]))
            P.need("t", io_tok)
            P.need("t", hm_tok)
            P.need("t", bank_free[id(pg[mp])])
            mmt = P.mile("t", lambda t, mp=mp: [
                t.matmul(pg[mp][:, 0:CH], wio[:, mp, k], hm[:, k],
                         start=(k == 0), stop=(k == KD - 1))
                for k in range(KD)
            ][-1])
            ring_tok[("io", mp)] = mmt
            GUARD["wout"] = mmt
            P.need("v", mmt)
            P.need("v", dump_tok)
            r2_tok = P.mile("v", lambda v, Mt=Mt, mp=mp, sl=sl: (
                spin_v(v),
                v.scalar_tensor_tensor(
                    sl[:, Mt], pg[mp][:, 0:CH],
                    b_out[:, Mt:Mt + 1], sl[:, Mt],
                    ALU.add, ALU.add),
            ) and None)
            bank_free[id(pg[mp])] = r2_tok
        slot_ready[chunk] = r2_tok
        if dbg:
            P.need("s", hm_tok)
            P.dma("s", lambda se, chunk=chunk: se.dma_start(
                out=dbg_t["dbg_hma"][chunk], in_=hm[:]))
            P.need("s", r2_tok)
            P.dma("s", lambda se, sl=sl, chunk=chunk: se.dma_start(
                out=dbg_t["dbg_r2a"][chunk], in_=sl))
        if dbg and s_ == 0:
            P.need("s", hm_tok)
            P.dma("s", lambda se: se.dma_start(out=dbg_t["dbg_hm"], in_=hm[:]))
            P.need("s", r2_tok)
            P.dma("s", lambda se, sl=sl: se.dma_start(out=dbg_t["dbg_r2"], in_=sl))

        P.need("v", xq_free[(s_ + 2) % 2])
        qc = emit_quant(sl, xq[:, (s_ + 2) % 2, :, CH:512], "ffn", r2_tok)
        xq_tok[(s_ + 2) % 2] = qc

        if s_ + 1 < NCHUNK:
            nxt = s_ + 1
            if nxt >= 3:
                P.need("s", slot_store[nxt - 3])
            lx = P.dma("s", lambda se, nxt=nxt: se.dma_start(
                out=slot[:, nxt % 3], in_=xslice(nxt)))
            slot_ready[nxt] = lx
            qa = emit_quant(slot[:, nxt % 3], xq[:, nxt % 2, :, 0:CH], "ffn", lx)
            xq_tok[nxt % 2] = qa

    P.items["s"].append(("w", "sd", P.cnt["sd"]))
    P.finalize()

    with es:
        with nc.Block() as block:
            def emit(stream):
                def body(eng):
                    for it in P.items[stream]:
                        if it[0] == "w":
                            eng.wait_ge(sems[it[1]], it[2])
                        elif it[0] == "c":
                            it[1](eng)
                        elif it[0] == "ci":
                            it[1](eng).then_inc(sems[it[2]], 1)
                        elif it[0] == "di":
                            it[1](eng).then_inc(sems[it[2]], 16)
                return body

            block.tensor(emit("t"))
            block.vector(emit("v"))
            block.scalar(emit("s"))
            block.gpsimd(emit("g"))
            block.sync(emit("q"))
    return nc


def kernel(L0_attn_x, L0_attn_W_in, L0_attn_b_in, L0_attn_W_gx, L0_attn_b_gx,
           L0_attn_W_ga, L0_attn_b_ga, L0_attn_a_real, L0_attn_a_imag,
           L0_attn_W_out, L0_attn_b_out, L0_ffn_W_g, L0_ffn_W_u, L0_ffn_W_d):
    x = np.asarray(L0_attn_x, dtype=np.float32)
    B = x.shape[0]
    nc = build_program()

    def pc(a, k):
        return np.ascontiguousarray(np.asarray(a, np.float32).reshape(k, 128).T)

    wgx = np.ascontiguousarray(
        np.asarray(L0_attn_W_gx, np.float32).reshape(8, 2, 128, 128)
        .transpose(2, 0, 1, 3).reshape(128, 16, 128))
    wga = np.ascontiguousarray(
        np.asarray(L0_attn_W_ga, np.float32).reshape(8, 2, 128, 128)
        .transpose(2, 0, 1, 3).reshape(128, 16, 128))
    common = {
        "w_in": np.ascontiguousarray(np.asarray(L0_attn_W_in, np.float32)),
        "w_out": np.ascontiguousarray(np.asarray(L0_attn_W_out, np.float32)),
        "w_g": np.ascontiguousarray(np.asarray(L0_ffn_W_g, np.float32)),
        "w_u": np.ascontiguousarray(np.asarray(L0_ffn_W_u, np.float32)),
        "w_d": np.ascontiguousarray(np.asarray(L0_ffn_W_d, np.float32)),
        "wgx": wgx, "wga": wga,
        "bgx": np.ascontiguousarray(np.asarray(L0_attn_b_gx, np.float32).T),
        "bga": np.ascontiguousarray(np.asarray(L0_attn_b_ga, np.float32).T),
        "bin": pc(L0_attn_b_in, 16),
        "bout": pc(L0_attn_b_out, 16),
        "arel": pc(L0_attn_a_real, 8),
        "aimg": pc(L0_attn_a_imag, 8),
    }
    in_maps = []
    for b in range(B):
        m = dict(common)
        m["x_fm"] = np.ascontiguousarray(x[b].T)
        in_maps.append(m)
    res = run_bass_kernel_spmd(nc, in_maps, core_ids=list(range(B)))
    out = np.stack([np.ascontiguousarray(np.asarray(r["y_fm"]).T)
                    for r in res.results])
    return out.astype(np.float32)
